# version 02 (final)
# baseline (speedup 1.0000x reference)
import math
from contextlib import ExitStack

import numpy as np

import concourse.bass as bass
import concourse.mybir as mybir
from concourse.bass_utils import run_bass_kernel_spmd

F32 = mybir.dt.float32
I16 = mybir.dt.int16
AF = mybir.ActivationFunctionType
ALU = mybir.AluOpType
AX = mybir.AxisListType

B, T, D = 2, 8192, 2048
NCORE = 8
NTOK = 2048
NT = 16
H0, DK, ROPE, NOPE, VD, KVR, QR = 16, 128, 64, 64, 128, 512, 768
SCALE0 = DK ** -0.5
EPS_RMS = float(np.finfo(np.float32).eps)
EPS_LN = 1e-5
E, TOPK, CAP, PEXP = 64, 6, 1024, 256
HS = 512
EH = 256
NH1, HD1 = 16, 128
SCALE1A = 1.0 / math.sqrt(2048.0)
SCALE1F = 1.0 / math.sqrt(HD1)
SHIFT = 8.0


def _core_tiles(c):
    i = c % 4
    return [8 * i + j for j in range(8)] + [56 - 8 * i + j for j in range(8)]


def _core_rows(c):
    b = c // 4
    out = []
    for g in _core_tiles(c):
        s = b * T + g * 128
        out.append(np.arange(s, s + 128))
    return np.concatenate(out)


def _halfblock_order():
    order = []
    for b in range(2):
        for i in range(4):
            order.append((b * 4 + i, 0))
        for i in range(3, -1, -1):
            order.append((b * 4 + i, 1))
    return order


def _carry_masks():
    order = _halfblock_order()
    pos = {ch: k for k, ch in enumerate(order)}
    masks = np.zeros((8, 2, 16), np.float32)
    for c in range(8):
        for h in range(2):
            for (c2, h2), k in pos.items():
                if k < pos[(c, h)]:
                    masks[c, h, c2 * 2 + h2] = 1.0
    return masks


def _rope_tables(positions):
    inv = 1.0 / 10000.0 ** (np.arange(0, ROPE, 2, dtype=np.float64) / ROPE)
    fr = np.outer(positions.astype(np.float64), inv)
    return np.cos(fr).astype(np.float32), np.sin(fr).astype(np.float32)



class EV:
    __slots__ = ("sem", "val")

    def __init__(self, sem, val):
        self.sem = sem
        self.val = val


_ENGINES = ("sync", "tensor", "vector", "scalar", "gpsimd")
_SEM_OWNER = {
    "ld": "sync", "sy": "sync", "mm": "tensor", "ev": "vector", "ac": "scalar",
    "ld2": "scalar", "st": "gpsimd", "gp": "gpsimd", "cc": "gpsimd",
}
_BAR_SEM = {"sync": "sy", "tensor": "mm", "vector": "ev", "scalar": "ac",
            "gpsimd": "gp"}


class P:

    def __init__(self, nc, stack):
        self.nc = nc
        self.stack = stack
        self.items = {e: [] for e in _ENGINES}
        self.sem_h = {n: stack.enter_context(nc.semaphore(n))
                      for n in _SEM_OWNER}
        self.cnt = {n: 0 for n in _SEM_OWNER}
        self.pb = [stack.enter_context(nc.psum_tensor(f"pb{i}", [128, 512], F32))
                   for i in range(8)]
        self.ident = None
        self.n_inst = 0
        self.last_bar = {}

    def wait(self, eng, ev):
        if ev is not None:
            val = ev.val
            if ev.sem in ("ld", "ld2", "st"):
                val = max(val, self.cnt[ev.sem])
            self.items[eng].append(("w", ev.sem, val))

    def wait_all(self, eng, sem):
        self.items[eng].append(("w", sem, self.cnt[sem]))

    def emit(self, eng, fn, inc=None, inc_n=1):
        ev = None
        if inc is not None:
            self.cnt[inc] += inc_n
            ev = EV(inc, self.cnt[inc])
        self.items[eng].append(["i", fn, inc, inc_n, ev])
        self.n_inst += 1
        return ev

    def mark(self, eng):
        for it in reversed(self.items[eng]):
            if it[0] == "i":
                if it[2] is None:
                    sem = _BAR_SEM[eng]
                    self.cnt[sem] += 1
                    it[2] = sem
                    it[3] = 1
                    it[4] = EV(sem, self.cnt[sem])
                return it[4]
        return self.last_bar.get(eng)

    def dma(self, eng, out, in_, sem=None):
        if sem is None:
            sem = {"sync": "ld", "scalar": "ld2", "gpsimd": "st"}[eng]
        return self.emit(eng, lambda E, o=out, i=in_: E.dma_start(out=o, in_=i),
                         inc=sem, inc_n=16)

    def barrier(self, engines=_ENGINES):
        marks = {e: self.mark(e) for e in engines}
        for e in engines:
            for e2, ev in marks.items():
                self.wait(e, ev)
            for s in ("ld", "ld2", "st", "cc"):
                self.wait_all(e, s)
        for e in engines:
            self.last_bar[e] = marks[e]

    def chain(self, eng):
        self.wait(eng, self.mark(eng))

    def flush(self):
        items = self.items
        self.items = {e: [] for e in _ENGINES}
        sem_h = self.sem_h

        def runner(lst):
            def run(engine):
                for it in lst:
                    if it[0] == "w":
                        engine.wait_ge(sem_h[it[1]], it[2])
                    else:
                        _, fn, inc, inc_n, _ev = it
                        inst = fn(engine)
                        if inc is not None:
                            inst.then_inc(sem_h[inc], inc_n)
            return run

        with self.nc.Block() as blk:
            for e in _ENGINES:
                if items[e]:
                    getattr(blk, e)(runner(items[e]))

    def sb(self, name, shape, dtype=F32):
        return self.stack.enter_context(self.nc.sbuf_tensor(name, shape, dtype))


def bcast_mid(ap2d, n):
    from concourse.bass import AP as _AP
    a = list(ap2d.ap)
    assert len(a) == 2
    return _AP(tensor=ap2d.tensor, offset=ap2d.offset,
               ap=[list(a[0]), [0, n], list(a[1])])



def ln_tile(p, x_ap, sq, stats, pre_ev, eps=EPS_LN, width=D, apply_pre=None):
    p.wait("vector", pre_ev)
    p.wait("scalar", pre_ev)
    p.emit("vector", lambda E, x=x_ap: E.reduce_sum(stats[:, 0:1], x, axis=AX.X))
    p.chain("vector")
    sq_ev = p.emit("scalar", lambda E, x=x_ap, w=width: E.activation(
        sq[:, 0:w], x, AF.Square, accum_out=stats[:, 1:2]), inc="ac")
    p.wait("vector", sq_ev)
    p.emit("vector", lambda E, w=width: E.tensor_scalar(
        stats[:, 2:3], stats[:, 0:1], 1.0 / w, None, op0=ALU.mult))
    p.emit("vector", lambda E, w=width: E.tensor_scalar(
        stats[:, 3:4], stats[:, 1:2], 1.0 / w, None, op0=ALU.mult))
    p.chain("vector")
    p.emit("vector", lambda E: E.tensor_tensor(
        stats[:, 4:5], stats[:, 2:3], stats[:, 2:3], op=ALU.mult))
    p.chain("vector")
    var_ev = p.emit("vector", lambda E: E.tensor_tensor(
        stats[:, 4:5], stats[:, 3:4], stats[:, 4:5], op=ALU.subtract), inc="ev")
    p.wait("scalar", var_ev)
    std_ev = p.emit("scalar", lambda E, eps=eps: E.activation(
        stats[:, 5:6], stats[:, 4:5], AF.Sqrt, bias=eps), inc="ac")
    p.wait("vector", std_ev)
    p.emit("vector", lambda E: E.reciprocal(stats[:, 6:7], stats[:, 5:6]))
    p.chain("vector")
    p.wait("vector", apply_pre)
    return p.emit("vector", lambda E, x=x_ap: E.tensor_scalar(
        x, x, stats[:, 2:3], stats[:, 6:7], op0=ALU.subtract, op1=ALU.mult),
        inc="ev")


def transposes(p, pairs, pre_ev=None, banks=(0, 1)):
    pb = p.pb
    p.wait("tensor", pre_ev)
    last = {b: None for b in banks}
    tr_evs = []
    n_pairs = len(pairs)
    for k, (s, d, n) in enumerate(pairs):
        bk = banks[k % len(banks)]
        p.wait("tensor", last[bk])
        tr_evs.append(p.emit("tensor", lambda E, s=s, bk=bk, n=n: E.transpose(
            out=pb[bk][0:n, 0:128], in_=s, identity=p.ident[:, :]), inc="mm"))
        if k >= 1:
            km = k - 1
            bkm = banks[km % len(banks)]
            sm, dm, nm = pairs[km]
            p.wait("vector", tr_evs[k])
            last[bkm] = p.emit("vector", lambda E, dm=dm, bkm=bkm, nm=nm:
                               E.tensor_copy(dm, pb[bkm][0:nm, 0:128]),
                               inc="ev")
    bk_last = banks[(n_pairs - 1) % len(banks)]
    bk_dummy = banks[n_pairs % len(banks)] if len(banks) > 1 else bk_last
    if bk_dummy == bk_last:
        dummy_out = pb[bk_last][0:128, 128:256]
    else:
        p.wait("tensor", last[bk_dummy])
        dummy_out = pb[bk_dummy][0:128, 0:128]
    dummy_ev = p.emit("tensor", lambda E, o=dummy_out: E.transpose(
        out=o, in_=p.ident[:, :], identity=p.ident[:, :]), inc="mm")
    sm, dm, nm = pairs[n_pairs - 1]
    p.wait("vector", dummy_ev)
    out_ev = p.emit("vector", lambda E, dm=dm, bk=bk_last, nm=nm:
                    E.tensor_copy(dm, pb[bk][0:nm, 0:128]), inc="ev")
    return out_ev


def rope_heads(p, buf, head_starts, ct, st_, rsc, pre_ev=None):
    n = len(head_starts)
    from concourse.bass import AP as _AP

    def strided(c0):
        a = list(buf[:, 0:1].ap)
        step_h = head_starts[1] - head_starts[0] if n > 1 else 0
        return _AP(tensor=buf[:, 0:1].tensor,
                   offset=buf[:, 0:1].offset + head_starts[0] + c0,
                   ap=[list(a[0]), [step_h, n], [1, 32]])

    x1 = strided(0)
    x2 = strided(32)
    cb = bcast_mid(ct, n)
    sb_ = bcast_mid(st_, n)
    t = [rsc[:, k * 32 * n:(k + 1) * 32 * n].rearrange("p (h c) -> p h c", h=n)
         for k in range(4)]
    p.wait("vector", pre_ev)
    p.emit("vector", lambda E: E.tensor_tensor(t[0], x1, cb, op=ALU.mult))
    p.emit("vector", lambda E: E.tensor_tensor(t[1], x2, sb_, op=ALU.mult))
    p.emit("vector", lambda E: E.tensor_tensor(t[2], x1, sb_, op=ALU.mult))
    p.emit("vector", lambda E: E.tensor_tensor(t[3], x2, cb, op=ALU.mult))
    p.chain("vector")
    p.emit("vector", lambda E: E.tensor_tensor(x1, t[0], t[1], op=ALU.subtract))
    ev = p.emit("vector", lambda E: E.tensor_tensor(x2, t[2], t[3], op=ALU.add),
                inc="ev")
    return ev



A1_NT = NT
A1_PARTS = "ln,tr,qa,kva,post,postB,postC,post2,rope,kpet,store"


def emit_phase_a1(p, io, hb):
    nc, pb = p.nc, p.pb
    parts = set(A1_PARTS.split(","))
    with ExitStack() as st:
        sb = lambda n, s, dt=F32: st.enter_context(nc.sbuf_tensor(n, s, dt))
        w_qa = sb("a1_wqa", [128, 16 * QR])
        w_kva = sb("a1_wkva", [128, 16 * 576])
        gq_rep = sb("a1_gqrep", [128, QR])
        gkv_rep = sb("a1_gkvrep", [128, KVR])
        cos_sb = sb("a1_cos", [128, NT * 32])
        sin_sb = sb("a1_sin", [128, NT * 32])
        x_sb = sb("a1_x", [128, D])
        lnfm = sb("a1_lnfm", [128, 16 * 128])
        sq = sb("a1_sq", [128, D])
        stats = sb("a1_stats", [128, 16])
        qa_sb = sb("a1_qa", [128, QR])
        ckv_sb = sb("a1_ckv", [128, 576])
        qlat_sb = sb("a1_qlat", [128, QR])
        cnorm_sb = sb("a1_cnorm", [128, KVR])
        kpeT_sb = sb("a1_kpeT", [128, 128])
        rsc = sb("a1_rsc", [128, 4 * 32])

        for dch in range(16):
            p.dma("sync", w_qa[:, dch * QR:(dch + 1) * QR],
                  io["wq_a"][dch * 128:(dch + 1) * 128, :])
            p.dma("sync", w_kva[:, dch * 576:(dch + 1) * 576],
                  io["wkv_a"][dch * 128:(dch + 1) * 128, :])
        p.dma("sync", cos_sb[:, :].rearrange("p (t f) -> p t f", t=NT),
              io["cosT"].rearrange("(t p) f -> p t f", p=128))
        p.dma("sync", sin_sb[:, :].rearrange("p (t f) -> p t f", t=NT),
              io["sinT"].rearrange("(t p) f -> p t f", p=128))
        p.dma("sync", gq_rep[:, :], io["gq_rep"][:, :])
        p.dma("sync", gkv_rep[:, :], io["gkv_rep"][:, :])
        p.barrier()

        for t in range(A1_NT):
            x_ev = p.dma("sync", x_sb[:, :], io["x0"][t * 128:(t + 1) * 128, :])
            ln_ev = ln_tile(p, x_sb[:, :], sq, stats, x_ev) if "ln" in parts else x_ev
            if "tr" not in parts:
                p.wait_all("gpsimd", "st")
                p.barrier()
                continue
            fm_ev = transposes(
                p, [(x_sb[:, d * 128:(d + 1) * 128],
                     lnfm[:, d * 128:(d + 1) * 128], 128) for d in range(16)],
                pre_ev=ln_ev)
            p.wait("tensor", fm_ev)
            if "qa" not in parts:
                p.wait_all("gpsimd", "st")
                p.barrier()
                continue
            for dch in range(16):
                p.emit("tensor", lambda E, d=dch: E.matmul(
                    out=pb[2][:, 0:384], lhsT=lnfm[:, d * 128:(d + 1) * 128],
                    rhs=w_qa[:, d * QR:d * QR + 384],
                    start=(d == 0), stop=(d == 15)))
                p.emit("tensor", lambda E, d=dch: E.matmul(
                    out=pb[3][:, 0:384], lhsT=lnfm[:, d * 128:(d + 1) * 128],
                    rhs=w_qa[:, d * QR + 384:(d + 1) * QR],
                    start=(d == 0), stop=(d == 15)))
            qa_mm = p.mark("tensor")
            if "kva" not in parts:
                p.wait_all("gpsimd", "st")
                p.barrier()
                continue
            for dch in range(16):
                p.emit("tensor", lambda E, d=dch: E.matmul(
                    out=pb[4][:, 0:288], lhsT=lnfm[:, d * 128:(d + 1) * 128],
                    rhs=w_kva[:, d * 576:d * 576 + 288],
                    start=(d == 0), stop=(d == 15)))
                p.emit("tensor", lambda E, d=dch: E.matmul(
                    out=pb[5][:, 0:288], lhsT=lnfm[:, d * 128:(d + 1) * 128],
                    rhs=w_kva[:, d * 576 + 288:(d + 1) * 576],
                    start=(d == 0), stop=(d == 15)))
            kva_mm = p.mark("tensor")

            if "post" not in parts:
                p.wait_all("gpsimd", "st")
                p.barrier()
                continue
            p.wait("scalar", qa_mm)
            p.emit("scalar", lambda E: E.copy(qa_sb[:, 0:384], pb[2][:, 0:384]))
            p.emit("scalar", lambda E: E.copy(qa_sb[:, 384:768],
                                              pb[3][:, 0:384]))
            p.chain("scalar")
            p.emit("scalar", lambda E: E.activation(
                sq[:, 0:QR], qa_sb[:, :], AF.Square, accum_out=stats[:, 7:8]))
            p.chain("scalar")
            qss = p.emit("scalar", lambda E: E.activation(
                stats[:, 8:9], stats[:, 7:8], AF.Sqrt, scale=1.0 / QR,
                bias=EPS_RMS), inc="ac")
            p.wait("vector", qss)
            p.emit("vector", lambda E: E.reciprocal(stats[:, 9:10],
                                                    stats[:, 8:9]))
            p.chain("vector")
            ql_ev = p.emit("vector", lambda E: E.scalar_tensor_tensor(
                qlat_sb[:, :], qa_sb[:, :], stats[:, 9:10], gq_rep[:, :],
                op0=ALU.mult, op1=ALU.mult), inc="ev")

            p.wait("scalar", kva_mm)
            p.emit("scalar", lambda E: E.copy(ckv_sb[:, 0:288],
                                              pb[4][:, 0:288]))
            ckv_ev = p.emit("scalar", lambda E: E.copy(
                ckv_sb[:, 288:576], pb[5][:, 0:288]), inc="ac")
            p.chain("scalar")
            p.emit("scalar", lambda E: E.activation(
                sq[:, 0:KVR], ckv_sb[:, 0:KVR], AF.Square,
                accum_out=stats[:, 10:11]))
            p.chain("scalar")
            css = p.emit("scalar", lambda E: E.activation(
                stats[:, 11:12], stats[:, 10:11], AF.Sqrt, scale=1.0 / KVR,
                bias=EPS_RMS), inc="ac")
            p.wait("vector", css)
            p.emit("vector", lambda E: E.reciprocal(stats[:, 12:13],
                                                    stats[:, 11:12]))
            p.chain("vector")
            cn_ev = p.emit("vector", lambda E: E.scalar_tensor_tensor(
                cnorm_sb[:, :], ckv_sb[:, 0:KVR], stats[:, 12:13],
                gkv_rep[:, :], op0=ALU.mult, op1=ALU.mult), inc="ev")

            rp_ev = rope_heads(p, ckv_sb, [KVR],
                               cos_sb[:, t * 32:(t + 1) * 32],
                               sin_sb[:, t * 32:(t + 1) * 32], rsc,
                               pre_ev=ckv_ev)
            kT_ev = transposes(p, [(ckv_sb[:, KVR:KVR + 64],
                                    kpeT_sb[0:64, 0:128], 64)],
                               pre_ev=rp_ev, banks=(6,))
            p.wait("gpsimd", ql_ev)
            p.dma("gpsimd", hb["qlat"][t * 128:(t + 1) * 128, :], qlat_sb[:, :])
            p.wait("gpsimd", cn_ev)
            p.dma("gpsimd", hb["cnorm"][t * 128:(t + 1) * 128, :],
                  cnorm_sb[:, :])
            p.wait("gpsimd", kT_ev)
            p.dma("gpsimd", hb["kpeT"][0:64, t * 128:(t + 1) * 128],
                  kpeT_sb[0:64, 0:128])
            p.wait_all("gpsimd", "st")
            p.barrier()
        p.flush()


A2_PARTS = "tra,qb,evq,rope,qtr,kvb,vst,ktr"


def emit_phase_a2(p, io, hb):
    nc, pb = p.nc, p.pb
    parts = set(A2_PARTS.split(","))
    with ExitStack() as st:
        sb = lambda n, s, dt=F32: st.enter_context(nc.sbuf_tensor(n, s, dt))
        w_qb = sb("a2_wqb", [128, 6 * 2048])
        w_kvb = sb("a2_wkvb", [128, 4 * 3072])
        cos_sb = sb("a2_cos", [128, NT * 32])
        sin_sb = sb("a2_sin", [128, NT * 32])
        qlat_sb = sb("a2_qlat", [128, QR])
        cnorm_sb = sb("a2_cnorm", [128, KVR])
        kpeT_sb = sb("a2_kpeT", [128, 128])
        qlfm = sb("a2_qlfm", [128, 6 * 128])
        cnfm = sb("a2_cnfm", [128, 4 * 128])
        q_sb = sb("a2_q", [128, 2048])
        kv_sb = sb("a2_kv", [128, 3072])
        kfmT = sb("a2_kfmT", [128, 16 * 128])
        qfmT = sb("a2_qfmT", [128, 16 * 128])
        rsc = sb("a2_rsc", [128, 4 * 32 * H0])

        for lch in range(6):
            p.dma("sync", w_qb[:, lch * 2048:(lch + 1) * 2048],
                  io["wq_b"][lch * 128:(lch + 1) * 128, :])
        for lch in range(4):
            p.dma("sync", w_kvb[:, lch * 3072:(lch + 1) * 3072],
                  io["wkv_b"][lch * 128:(lch + 1) * 128, :])
        p.dma("sync", cos_sb[:, :].rearrange("p (t f) -> p t f", t=NT),
              io["cosT"].rearrange("(t p) f -> p t f", p=128))
        p.dma("sync", sin_sb[:, :].rearrange("p (t f) -> p t f", t=NT),
              io["sinT"].rearrange("(t p) f -> p t f", p=128))
        p.barrier()

        for t in range(NT):
            ql_ev = p.dma("sync", qlat_sb[:, :],
                          hb["qlat"][t * 128:(t + 1) * 128, :])
            cn_ev = p.dma("sync", cnorm_sb[:, :],
                          hb["cnorm"][t * 128:(t + 1) * 128, :])
            kp_ev = p.dma("sync", kpeT_sb[0:64, 0:128],
                          hb["kpeT"][0:64, t * 128:(t + 1) * 128])
            if "tra" not in parts:
                p.barrier()
                continue
            qlfm_ev = transposes(
                p, [(qlat_sb[:, c * 128:(c + 1) * 128],
                     qlfm[:, c * 128:(c + 1) * 128], 128) for c in range(6)],
                pre_ev=ql_ev)
            cnfm_ev = transposes(
                p, [(cnorm_sb[:, c * 128:(c + 1) * 128],
                     cnfm[:, c * 128:(c + 1) * 128], 128) for c in range(4)],
                pre_ev=cn_ev, banks=(2, 3))
            if "qb" not in parts:
                p.barrier()
                continue
            p.wait("tensor", qlfm_ev)
            p.wait("tensor", cnfm_ev)
            for lch in range(6):
                for piece in range(4):
                    p.emit("tensor", lambda E, l=lch, pc=piece: E.matmul(
                        out=pb[2 + pc][:, 0:512],
                        lhsT=qlfm[:, l * 128:(l + 1) * 128],
                        rhs=w_qb[:, l * 2048 + pc * 512:
                                 l * 2048 + (pc + 1) * 512],
                        start=(l == 0), stop=(l == 5)))
            qb_mm = p.mark("tensor")
            if "evq" not in parts:
                p.barrier()
                continue
            p.wait("vector", qb_mm)
            for piece in range(4):
                p.emit("vector", lambda E, pc=piece: E.tensor_copy(
                    q_sb[:, pc * 512:(pc + 1) * 512], pb[2 + pc][:, 0:512]))
            q_ev = p.mark("vector")
            ct = cos_sb[:, t * 32:(t + 1) * 32]
            st_ = sin_sb[:, t * 32:(t + 1) * 32]
            p.chain("vector")
            rp_ev = rope_heads(p, q_sb, [h * 128 + 64 for h in range(H0)],
                               ct, st_, rsc, pre_ev=None)
            p.wait("gpsimd", rp_ev)
            p.dma("gpsimd", hb["qtok"][t * 128:(t + 1) * 128, :], q_sb[:, :])
            qfm_ev = transposes(
                p, [(q_sb[:, h * 128:(h + 1) * 128],
                     qfmT[:, h * 128:(h + 1) * 128], 128) for h in range(H0)],
                pre_ev=rp_ev, banks=(6, 7))
            p.wait("gpsimd", qfm_ev)
            p.dma("gpsimd",
                  hb["qfm"][:, :, t * 128:(t + 1) * 128]
                      .rearrange("h d c -> d h c"),
                  qfmT[:, :].rearrange("d (h c) -> d h c", h=H0))
            if "kvb" not in parts:
                p.barrier()
                continue
            p.wait("tensor", q_ev)
            p.wait("tensor", cnfm_ev)
            for grp in range(2):
                for lch in range(4):
                    for piece in range(3):
                        pc = grp * 3 + piece
                        p.emit("tensor", lambda E, l=lch, pc=pc, b=piece:
                               E.matmul(
                                   out=pb[2 + b][:, 0:512],
                                   lhsT=cnfm[:, l * 128:(l + 1) * 128],
                                   rhs=w_kvb[:, l * 3072 + pc * 512:
                                             l * 3072 + (pc + 1) * 512],
                                   start=(l == 0), stop=(l == 3)))
                g_mm = p.mark("tensor")
                p.wait("vector", g_mm)
                last = None
                for piece in range(3):
                    pc = grp * 3 + piece
                    last = p.emit("vector", lambda E, pc=pc, b=piece:
                                  E.tensor_copy(kv_sb[:, pc * 512:(pc + 1) * 512],
                                                pb[2 + b][:, 0:512]), inc="ev")
                p.wait("tensor", last)
            kv_ev = last
            if "vst" in parts:
                p.wait("gpsimd", kv_ev)
                p.dma("gpsimd", hb["v_in"][t * 128:(t + 1) * 128, :],
                      kv_sb[:, :].rearrange("p (h x) -> p h x", h=H0)[:, :, 64:192])
            if "ktr" not in parts:
                p.barrier()
                continue
            knT_ev = transposes(
                p, [(kv_sb[:, h * 192:h * 192 + 64],
                     kfmT[0:64, h * 128:(h + 1) * 128], 64) for h in range(H0)],
                pre_ev=kv_ev, banks=(6, 7))
            p.wait("vector", kp_ev)
            for h in range(H0):
                p.emit("vector", lambda E, h=h: E.tensor_copy(
                    kfmT[64:128, h * 128:(h + 1) * 128], kpeT_sb[0:64, 0:128]))
            kfm_ev = p.mark("vector")
            p.wait("gpsimd", knT_ev)
            p.wait("gpsimd", kfm_ev)
            p.dma("gpsimd",
                  hb["kfm_in"][:, t * 128:(t + 1) * 128]
                      .rearrange("(h r) c -> r h c", h=H0),
                  kfmT[:, :].rearrange("r (h c) -> r h c", h=H0))
            p.wait_all("gpsimd", "st")
            p.barrier()
        p.flush()


def emit_ag_l0(p, hb):
    p.wait_all("gpsimd", "st")
    p.emit("gpsimd", lambda E: E.collective_compute(
        "AllGather", ALU.bypass,
        replica_groups=[[0, 1, 2, 3], [4, 5, 6, 7]],
        ins=[hb["kfm_in"][:, :].opt()],
        outs=[hb["kfm"].rearrange("r d t -> (r d) t").opt()]), inc="cc")
    cc2 = p.emit("gpsimd", lambda E: E.collective_compute(
        "AllGather", ALU.bypass,
        replica_groups=[[0, 1, 2, 3], [4, 5, 6, 7]],
        ins=[hb["v_in"][:, :].opt()],
        outs=[hb["v"].rearrange("r t d -> (r t) d").opt()]), inc="cc")
    p.wait("gpsimd", cc2)
    p.barrier()
    p.flush()





def emit_phase_b(p, io, hb):
    nc, pb = p.nc, p.pb
    with ExitStack() as st:
        sb = lambda n, s, dt=F32: st.enter_context(nc.sbuf_tensor(n, s, dt))
        kh = sb("b_kh", [128, 8192])
        vh = sb("b_vh", [128, 64 * 129])
        qh = sb("b_qh", [128, 2048])
        probs = sb("b_probs", [128, 2 * 512])
        attn = sb("b_attn", [128, 16 * 128])
        attnT = sb("b_attnT", [128, 16 * 128])
        rr = sb("b_rr", [128, 8])

        ones_ap = vh[:, :].rearrange("p (t c) -> p t c", t=64)[:, :, 128:129]
        p.emit("vector", lambda E: E.memset(ones_ap, 1.0))
        p.barrier()

        for h in range(H0):
            for r in range(4):
                p.dma("sync", kh[:, r * 2048:(r + 1) * 2048],
                      hb["kfm"][r, h * 128:(h + 1) * 128, :])
                p.dma("sync",
                      vh[:, :].rearrange("p (t c) -> p t c", t=64)
                          [:, r * 16:(r + 1) * 16, 0:128],
                      hb["v"][r, :, h * 128:(h + 1) * 128]
                          .rearrange("(t q) c -> q t c", q=128))
            qld = p.dma("sync", qh[:, :], hb["qfm"][h, :, :])
            p.wait("tensor", qld)

            for qp in range(4):
                sc_ev = [None, None]
                ex_ev = [None, None]
                av_ev = [None, None]
                for kt in range(64):
                    par = kt % 2
                    p.wait("tensor", ex_ev[par])
                    sc_ev[par] = p.emit("tensor", lambda E, kt=kt, qp=qp,
                                        par=par: E.matmul(
                        out=pb[par][:, 0:512],
                        lhsT=kh[:, kt * 128:(kt + 1) * 128],
                        rhs=qh[:, qp * 512:(qp + 1) * 512],
                        start=True, stop=True), inc="mm")
                    p.wait("scalar", sc_ev[par])
                    p.wait("scalar", av_ev[par])
                    ex_ev[par] = p.emit("scalar", lambda E, par=par: E.activation(
                        probs[:, par * 512:(par + 1) * 512], pb[par][:, 0:512],
                        AF.Exp, scale=SCALE0, bias=-SHIFT), inc="ac")
                    p.wait("tensor", ex_ev[par])
                    for qs in range(4):
                        p.emit("tensor", lambda E, kt=kt, par=par, qs=qs:
                               E.matmul(
                            out=pb[2 + qs][:, 0:129],
                            lhsT=probs[:, par * 512 + qs * 128:
                                       par * 512 + (qs + 1) * 128],
                            rhs=vh[:, kt * 129:(kt + 1) * 129],
                            start=(kt == 0), stop=(kt == 63)))
                    av_ev[par] = p.mark("tensor")
                p.wait("vector", av_ev[(64 - 1) % 2])
                for qs in range(4):
                    p.emit("vector", lambda E, qs=qs: E.reciprocal(
                        rr[:, qs:qs + 1], pb[2 + qs][:, 128:129]))
                p.chain("vector")
                nrm = None
                for qs in range(4):
                    nrm = p.emit("vector", lambda E, qs=qs, qp=qp:
                                 E.tensor_scalar(
                        attn[:, (qp * 4 + qs) * 128:(qp * 4 + qs + 1) * 128],
                        pb[2 + qs][:, 0:128], rr[:, qs:qs + 1], None,
                        op0=ALU.mult), inc="ev")
                p.wait("tensor", nrm)
            aT_ev = transposes(
                p, [(attn[:, k * 128:(k + 1) * 128],
                     attnT[:, k * 128:(k + 1) * 128], 128) for k in range(16)],
                pre_ev=nrm, banks=(6, 7))
            p.wait("gpsimd", aT_ev)
            p.dma("gpsimd",
                  hb["attnT"][h].rearrange("v (t c) -> v t c", t=16),
                  attnT[:, :].rearrange("v (t c) -> v t c", t=16))
            p.barrier()
        p.flush()


def emit_phase_b2(p, io, hb):
    emit_proj_res_ln(p, io, hb,
                     att_src=hb["attnT"], w_in=io["wo"], w_cols=2048,
                     res_src=io["x0"], out_x=hb["x1"], out_lntok=hb["ln1tok"],
                     out_lnfm=hb["ln1fm"], bias_col=None)


_PRL_N = [0]


def emit_proj_res_ln(p, io, hb, att_src, w_in, w_cols, res_src, out_x,
                     out_lntok, out_lnfm, bias_col=None, bias_io=None):
    nc, pb = p.nc, p.pb
    _PRL_N[0] += 1
    tagp = f"prl{_PRL_N[0]}"
    with ExitStack() as st:
        sb = lambda n, s, dt=F32: st.enter_context(
            nc.sbuf_tensor(tagp + n, s, dt))
        w_sb = sb("pr_w", [128, 16 * 2048])
        a_sb = sb("pr_a", [128, 16 * 128])
        x_sb = sb("pr_x", [128, 2048])
        r_sb = sb("pr_r", [128, 2048])
        sq = sb("pr_sq", [128, 2048])
        stats = sb("pr_stats", [128, 16])
        lnfm = sb("pr_lnfm", [128, 16 * 128])
        bcol = sb("pr_bcol", [128, 2048]) if bias_io is not None else None

        for ch in range(16):
            p.dma("sync", w_sb[:, ch * 2048:(ch + 1) * 2048],
                  w_in[ch * 128:(ch + 1) * 128, :])
        if bias_io is not None:
            p.dma("sync", bcol[:, :], bias_io[:, :])
        p.barrier()

        for t in range(NT):
            a_ev = p.dma("sync", a_sb[:, :].rearrange("p (h c) -> p h c", h=16),
                         att_src[:, :, t * 128:(t + 1) * 128]
                             .rearrange("h v c -> v h c"))
            r_ev = p.dma("sync", r_sb[:, :], res_src[t * 128:(t + 1) * 128, :])
            p.wait("tensor", a_ev)
            for h in range(16):
                for piece in range(4):
                    p.emit("tensor", lambda E, h=h, piece=piece: E.matmul(
                        out=pb[2 + piece][:, 0:512],
                        lhsT=a_sb[:, h * 128:(h + 1) * 128],
                        rhs=w_sb[:, h * 2048 + piece * 512:
                                 h * 2048 + (piece + 1) * 512],
                        start=(h == 0), stop=(h == 15)))
            mm_ev = p.mark("tensor")
            p.wait("vector", mm_ev)
            p.wait("vector", r_ev)
            for piece in range(4):
                p.emit("vector", lambda E, piece=piece: E.tensor_tensor(
                    x_sb[:, piece * 512:(piece + 1) * 512],
                    pb[2 + piece][:, 0:512],
                    r_sb[:, piece * 512:(piece + 1) * 512], op=ALU.add))
            if bias_io is not None:
                p.chain("vector")
                for piece in range(4):
                    p.emit("vector", lambda E, piece=piece: E.tensor_tensor(
                        x_sb[:, piece * 512:(piece + 1) * 512],
                        x_sb[:, piece * 512:(piece + 1) * 512],
                        bcol[:, piece * 512:(piece + 1) * 512], op=ALU.add))
            x_ev = p.mark("vector")
            p.wait("gpsimd", x_ev)
            st_x = p.dma("gpsimd", out_x[t * 128:(t + 1) * 128, :], x_sb[:, :])
            ln_ev = ln_tile(p, x_sb[:, :], sq, stats, x_ev, apply_pre=st_x)
            p.wait("gpsimd", ln_ev)
            p.dma("gpsimd", out_lntok[t * 128:(t + 1) * 128, :], x_sb[:, :])
            fm_ev = transposes(
                p, [(x_sb[:, d * 128:(d + 1) * 128],
                     lnfm[:, d * 128:(d + 1) * 128], 128) for d in range(16)],
                pre_ev=ln_ev)
            p.wait("gpsimd", fm_ev)
            p.dma("gpsimd",
                  out_lnfm[:, :, t * 128:(t + 1) * 128]
                      .rearrange("d p c -> p d c"),
                  lnfm[:, :].rearrange("p (d c) -> p d c", d=16))
            p.barrier()
        p.flush()



C_SKIP_EXPERTS = False


def emit_phase_c(p, io, hb):
    nc, pb = p.nc, p.pb
    BIGNEG = -1.0e30

    with ExitStack() as st:
        sb = lambda n, s, dt=F32: st.enter_context(nc.sbuf_tensor(n, s, dt))
        I32 = mybir.dt.int32
        wg_sb = sb("c_wg", [128, 16 * E])
        a_sb = sb("c_a", [128, 16 * 128])
        xtok = sb("c_xtok", [128, D])
        scores = sb("c_scores", [128, E])
        cur = sb("c_cur", [128, E])
        neg_sb = sb("c_neg", [128, E])
        eq_all = sb("c_eq", [128, NT * TOPK * E])
        M_all = sb("c_M", [128, NT * E])
        G_all = sb("c_G", [128, NT * E])
        slot_all = sb("c_slot", [128, NT * E])
        ek_all = sb("c_ek", [128, NT * TOPK])
        iota64 = sb("c_iota", [128, E])
        ones_sb = sb("c_ones", [128, 128])
        ut_sb = sb("c_ut", [128, 128])
        cmT = sb("c_cmT", [16, 2])
        cnts16 = sb("c_cnts16", [16, E])
        rowbuf = sb("c_rows", [1, 8 * E])
        colsums = sb("c_colsums", [1, NT * E])
        counts2 = sb("c_counts2", [1, 2 * E])
        tmp_sb = sb("c_tmp", [128, E])
        wsum = sb("c_wsum", [128, 4])
        kcols = sb("c_kcols", [128, 8])
        gxi = sb("c_gxi", [128, 1], I32)
        pcol = sb("c_pcol", [128, 1])
        fill_sb = sb("c_fill", [128, 256])

        for ch in range(16):
            p.dma("sync", wg_sb[:, ch * E:(ch + 1) * E],
                  io["w_gate"][ch * 128:(ch + 1) * 128, :])
        p.dma("sync", iota64[:, :], io["iota64_rep"][:, :])
        p.dma("sync", ut_sb[:, :], io["ut_strict"][:, :])
        p.dma("sync", cmT[:, :], io["carry_maskT"][:, :])
        p.dma("sync", pcol[:, :], io["pcol"][:, :])
        p.emit("vector", lambda E_: E_.memset(neg_sb[:, :], BIGNEG))
        p.emit("vector", lambda E_: E_.memset(ones_sb[:, :], 1.0))
        p.emit("vector", lambda E_: E_.memset(counts2[:, :], 0.0))
        p.emit("vector", lambda E_: E_.memset(rowbuf[:, :], 0.0))
        p.emit("vector", lambda E_: E_.memset(fill_sb[:, 0:128], 99999.0))
        p.emit("vector", lambda E_: E_.memset(fill_sb[:, 128:256], 0.0))
        p.chain("vector")
        rbc = nc.alloc_register(mybir.EngineType.Pool, "c_bc")
        p.emit("gpsimd", lambda E_: nc.reg_mov(rbc, E * PEXP - 1))
        fz = p.mark("vector")
        p.wait("gpsimd", fz)
        p.dma("gpsimd", hb["lists"].rearrange("(p f) o -> p (f o)", p=128),
              fill_sb[:, 0:128])
        p.dma("gpsimd", hb["wlist"].rearrange("(p f) o -> p (f o)", p=128),
              fill_sb[:, 128:256])
        p.barrier()

        for t in range(NT):
            a_ev = p.dma("sync", a_sb[:, :].rearrange("p (d c) -> p d c", d=16),
                         hb["ln1fm"][:, :, t * 128:(t + 1) * 128]
                             .rearrange("d p c -> p d c"))
            p.wait("tensor", a_ev)
            for ch in range(16):
                p.emit("tensor", lambda E_, ch=ch: E_.matmul(
                    out=pb[2][:, 0:E], lhsT=a_sb[:, ch * 128:(ch + 1) * 128],
                    rhs=wg_sb[:, ch * E:(ch + 1) * E],
                    start=(ch == 0), stop=(ch == 15)))
            g_mm = p.mark("tensor")
            p.wait("scalar", g_mm)
            sig = p.emit("scalar", lambda E_: E_.activation(
                scores[:, :], pb[2][:, 0:E], AF.Sigmoid), inc="ac")
            p.wait("vector", sig)
            p.emit("vector", lambda E_: E_.tensor_copy(cur[:, :], scores[:, :]))
            p.emit("vector", lambda E_, t=t: E_.memset(
                M_all[:, t * E:(t + 1) * E], 0.0))
            p.chain("vector")
            for k in range(TOPK):
                eq = eq_all[:, (t * TOPK + k) * E:(t * TOPK + k + 1) * E]
                p.emit("vector", lambda E_: E_.reduce_max(
                    wsum[:, 1:2], cur[:, :], axis=AX.X))
                p.chain("vector")
                p.emit("vector", lambda E_, eq=eq: E_.tensor_scalar(
                    eq, cur[:, :], wsum[:, 1:2], None, op0=ALU.is_equal))
                if k == 0:
                    p.emit("vector", lambda E_: E_.tensor_copy(
                        wsum[:, 0:1], wsum[:, 1:2]))
                else:
                    p.emit("vector", lambda E_: E_.tensor_tensor(
                        wsum[:, 0:1], wsum[:, 0:1], wsum[:, 1:2], op=ALU.add))
                p.chain("vector")
                p.emit("vector", lambda E_, eq=eq: E_.tensor_tensor(
                    tmp_sb[:, :], eq, iota64[:, :], op=ALU.mult))
                p.emit("vector", lambda E_, t=t, k=k, eq=eq: E_.tensor_tensor(
                    M_all[:, t * E:(t + 1) * E], M_all[:, t * E:(t + 1) * E],
                    eq, op=ALU.add))
                p.chain("vector")
                p.emit("vector", lambda E_, t=t, k=k: E_.reduce_sum(
                    ek_all[:, t * TOPK + k:t * TOPK + k + 1], tmp_sb[:, :],
                    axis=AX.X))
                p.chain("vector")
                p.emit("vector", lambda E_, eq=eq: E_.tensor_scalar(
                    tmp_sb[:, :], eq, 1.0e30, None, op0=ALU.mult))
                p.chain("vector")
                p.emit("vector", lambda E_: E_.tensor_tensor(
                    cur[:, :], cur[:, :], tmp_sb[:, :], op=ALU.subtract))
                p.chain("vector")
            p.emit("vector", lambda E_: E_.reciprocal(wsum[:, 2:3],
                                                      wsum[:, 0:1]))
            p.chain("vector")
            p.emit("vector", lambda E_, t=t: E_.tensor_tensor(
                G_all[:, t * E:(t + 1) * E], M_all[:, t * E:(t + 1) * E],
                scores[:, :], op=ALU.mult))
            p.chain("vector")
            gn_ev = p.emit("vector", lambda E_, t=t: E_.tensor_scalar(
                G_all[:, t * E:(t + 1) * E], G_all[:, t * E:(t + 1) * E],
                wsum[:, 2:3], None, op0=ALU.mult), inc="ev")
            p.wait("tensor", gn_ev)
            cs_mm = p.emit("tensor", lambda E_, t=t: E_.matmul(
                out=pb[6][0:1, 0:E], lhsT=ones_sb[:, 0:1],
                rhs=M_all[:, t * E:(t + 1) * E], start=True, stop=True),
                inc="mm")
            p.wait("vector", cs_mm)
            p.emit("vector", lambda E_, t=t: E_.tensor_copy(
                colsums[0:1, t * E:(t + 1) * E], pb[6][0:1, 0:E]))
            half = 0 if t < 8 else 1
            p.emit("vector", lambda E_, half=half: E_.tensor_tensor(
                counts2[0:1, half * E:(half + 1) * E],
                counts2[0:1, half * E:(half + 1) * E],
                pb[6][0:1, 0:E], op=ALU.add))
            p.chain("vector")
            p.barrier()

        p.dma("gpsimd", hb["cnt_in"].rearrange("h e -> (h e)"),
              counts2[0:1, :])
        p.wait_all("gpsimd", "st")
        cc = p.emit("gpsimd", lambda E_: E_.collective_compute(
            "AllGather", ALU.bypass,
            replica_groups=[[0, 1, 2, 3, 4, 5, 6, 7]],
            ins=[hb["cnt_in"][:, :].opt()],
            outs=[hb["cnt_all"].rearrange("r h e -> (r h) e").opt()]), inc="cc")
        p.wait("gpsimd", cc)
        p.barrier()
        c_ev = p.dma("sync", cnts16[:, :],
                     hb["cnt_all"].rearrange("r h e -> (r h) e"))
        p.wait("tensor", c_ev)
        p.emit("tensor", lambda E_: E_.matmul(
            out=pb[6][0:1, 0:E], lhsT=cmT[:, 0:1], rhs=cnts16[:, :],
            start=True, stop=True))
        ca_mm = p.emit("tensor", lambda E_: E_.matmul(
            out=pb[7][0:1, 0:E], lhsT=cmT[:, 1:2], rhs=cnts16[:, :],
            start=True, stop=True), inc="mm")
        p.wait("vector", ca_mm)
        p.emit("vector", lambda E_: E_.tensor_copy(rowbuf[0:1, 0:E],
                                                   pb[6][0:1, 0:E]))
        p.emit("vector", lambda E_: E_.tensor_copy(rowbuf[0:1, E:2 * E],
                                                   pb[7][0:1, 0:E]))
        p.chain("vector")
        p.emit("vector", lambda E_: E_.tensor_tensor(
            rowbuf[0:1, 2 * E:3 * E], rowbuf[0:1, E:2 * E],
            counts2[0:1, 0:E], op=ALU.subtract))
        p.emit("vector", lambda E_: E_.memset(rowbuf[0:1, 3 * E:4 * E], 0.0))
        p.chain("vector")
        p.barrier()

        for t in range(NT):
            x_ev = p.dma("sync", xtok[:, :],
                         hb["ln1tok"][t * 128:(t + 1) * 128, :])
            half = 0 if t < 8 else 1
            Mt = M_all[:, t * E:(t + 1) * E]
            Gt = G_all[:, t * E:(t + 1) * E]
            slott = slot_all[:, t * E:(t + 1) * E]
            p.emit("tensor", lambda E_, Mt=Mt: E_.matmul(
                out=pb[2][:, 0:E], lhsT=ut_sb[:, :], rhs=Mt,
                start=True, stop=False))
            s_mm = p.emit("tensor", lambda E_: E_.matmul(
                out=pb[2][:, 0:E], lhsT=ones_sb[0:1, 0:128],
                rhs=rowbuf[0:1, 3 * E:4 * E], start=False, stop=False),
                inc="mm")
            p.wait("vector", s_mm)
            sl_ev = p.emit("vector", lambda E_, slott=slott: E_.tensor_copy(
                slott, pb[2][:, 0:E]), inc="ev")
            p.wait("tensor", sl_ev)
            adj_off = (0 if half == 0 else 2 * E)
            r_mm = p.emit("tensor", lambda E_, adj_off=adj_off: E_.matmul(
                out=pb[2][:, 0:E], lhsT=ones_sb[0:1, 0:128],
                rhs=rowbuf[0:1, adj_off:adj_off + E], start=False, stop=True),
                inc="mm")
            p.wait("vector", r_mm)
            p.emit("vector", lambda E_: E_.tensor_scalar(
                tmp_sb[:, :], pb[2][:, 0:E], float(CAP) - 0.5, None,
                op0=ALU.is_lt))
            p.chain("vector")
            p.emit("vector", lambda E_, Gt=Gt: E_.tensor_tensor(
                Gt, Gt, tmp_sb[:, :], op=ALU.mult))
            p.emit("vector", lambda E_, slott=slott: E_.tensor_scalar(
                tmp_sb[:, :], slott, float(PEXP) - 0.5, None, op0=ALU.is_lt))
            p.chain("vector")
            p.emit("vector", lambda E_, Gt=Gt: E_.tensor_tensor(
                Gt, Gt, tmp_sb[:, :], op=ALU.mult))
            p.emit("vector", lambda E_, t=t: E_.tensor_tensor(
                rowbuf[0:1, 3 * E:4 * E], rowbuf[0:1, 3 * E:4 * E],
                colsums[0:1, t * E:(t + 1) * E], op=ALU.add))
            p.chain("vector")
            p.wait("gpsimd", x_ev)
            for k in range(TOPK):
                eq = eq_all[:, (t * TOPK + k) * E:(t * TOPK + k + 1) * E]
                ekc = ek_all[:, t * TOPK + k:t * TOPK + k + 1]
                p.emit("vector", lambda E_, eq=eq, slott=slott:
                       E_.tensor_tensor(tmp_sb[:, :], slott, eq, op=ALU.mult))
                p.chain("vector")
                p.emit("vector", lambda E_: E_.reduce_sum(
                    kcols[:, 0:1], tmp_sb[:, :], axis=AX.X))
                p.emit("vector", lambda E_, eq=eq, Gt=Gt: E_.tensor_tensor(
                    tmp_sb[:, :], Gt, eq, op=ALU.mult))
                p.chain("vector")
                p.emit("vector", lambda E_: E_.reduce_sum(
                    kcols[:, 1:2], tmp_sb[:, :], axis=AX.X))
                p.emit("vector", lambda E_, ekc=ekc: E_.tensor_scalar(
                    kcols[:, 2:3], ekc, float(PEXP), None, op0=ALU.mult))
                p.emit("vector", lambda E_: E_.tensor_scalar(
                    kcols[:, 3:4], kcols[:, 0:1], float(PEXP) - 0.5, None,
                    op0=ALU.is_ge))
                p.chain("vector")
                p.emit("vector", lambda E_: E_.tensor_tensor(
                    kcols[:, 2:3], kcols[:, 2:3], kcols[:, 0:1], op=ALU.add))
                p.emit("vector", lambda E_: E_.tensor_scalar(
                    kcols[:, 3:4], kcols[:, 3:4], 1.0e9, None, op0=ALU.mult))
                p.chain("vector")
                p.emit("vector", lambda E_: E_.tensor_tensor(
                    kcols[:, 2:3], kcols[:, 2:3], kcols[:, 3:4], op=ALU.add))
                p.emit("vector", lambda E_, t=t: E_.tensor_scalar(
                    kcols[:, 4:5], pcol[:, :], float(t * 128), None,
                    op0=ALU.add))
                p.chain("vector")
                gx_ev = p.emit("vector", lambda E_: E_.tensor_copy(
                    gxi[:, :], kcols[:, 2:3]), inc="ev")
                p.wait("gpsimd", gx_ev)
                from concourse.bass_core_rust import IndirectOffsetOnAxis
                p.emit("gpsimd", lambda E_: E_.indirect_dma_start(
                    hb["xdisp"][:, :], IndirectOffsetOnAxis(gxi[:, 0:1], 0),
                    xtok[:, :], None, bounds_check=rbc,
                    oob_is_err=False), inc="st", inc_n=16)
                p.emit("gpsimd", lambda E_: E_.indirect_dma_start(
                    hb["lists"][:, :], IndirectOffsetOnAxis(gxi[:, 0:1], 0),
                    kcols[:, 4:5], None, bounds_check=rbc,
                    oob_is_err=False), inc="st", inc_n=16)
                p.emit("gpsimd", lambda E_: E_.indirect_dma_start(
                    hb["wlist"][:, :], IndirectOffsetOnAxis(gxi[:, 0:1], 0),
                    kcols[:, 1:2], None, bounds_check=rbc,
                    oob_is_err=False), inc="st", inc_n=16)
                p.wait_all("gpsimd", "st")
                p.wait_all("vector", "st")
            p.barrier()
        p.flush()

    with ExitStack() as st:
        sb = lambda n, s, dt=F32: st.enter_context(nc.sbuf_tensor(n, s, dt))
        wsup = sb("c4_wsup", [128, 16 * 2 * HS])
        wsdn = sb("c4_wsdn", [128, 4 * D])
        lnp = sb("c4_lnp", [128, 16 * 512])
        hs_sb = sb("c4_hs", [128, 4 * 512])
        sil = sb("c4_sil", [128, 512])
        x1t = sb("c4_x1", [128, 2048])
        acc = sb("c4_acc", [128, 2048])

        for ch in range(16):
            p.dma("sync", wsup[:, ch * 2 * HS:(ch + 1) * 2 * HS],
                  io["w_sup"][ch * 128:(ch + 1) * 128, :])
        for ch in range(4):
            p.dma("sync", wsdn[:, ch * D:(ch + 1) * D],
                  io["w_sdn"][ch * 128:(ch + 1) * 128, :])
        p.barrier()

        for panel in range(4):
            lnp_ev = None
            for dch in range(16):
                lnp_ev = p.dma("sync", lnp[:, dch * 512:(dch + 1) * 512],
                               hb["ln1fm"][dch, :, panel * 512:(panel + 1) * 512])
            p.wait("tensor", lnp_ev)
            for pair in range(4):
                for dch in range(16):
                    p.emit("tensor", lambda E_, dch=dch, pair=pair: E_.matmul(
                        out=pb[2][:, 0:512],
                        lhsT=wsup[:, dch * 1024 + pair * 128:
                                  dch * 1024 + (pair + 1) * 128],
                        rhs=lnp[:, dch * 512:(dch + 1) * 512],
                        start=(dch == 0), stop=(dch == 15)))
                for dch in range(16):
                    p.emit("tensor", lambda E_, dch=dch, pair=pair: E_.matmul(
                        out=pb[3][:, 0:512],
                        lhsT=wsup[:, dch * 1024 + (pair + 4) * 128:
                                  dch * 1024 + (pair + 5) * 128],
                        rhs=lnp[:, dch * 512:(dch + 1) * 512],
                        start=(dch == 0), stop=(dch == 15)))
                up_mm = p.mark("tensor")
                p.wait("scalar", up_mm)
                sl_ev = p.emit("scalar", lambda E_: E_.activation(
                    sil[:, :], pb[3][:, 0:512], AF.Silu), inc="ac")
                p.wait("vector", sl_ev)
                hs_ev = p.emit("vector", lambda E_, pair=pair: E_.tensor_tensor(
                    hs_sb[:, pair * 512:(pair + 1) * 512], sil[:, :],
                    pb[2][:, 0:512], op=ALU.mult), inc="ev")
                p.wait("tensor", hs_ev)
            for tt4 in range(4):
                t = panel * 4 + tt4
                x_ev = p.dma("sync", x1t[:, :],
                             hb["x1"][t * 128:(t + 1) * 128, :])
                for piece in range(4):
                    for hch in range(4):
                        p.emit("tensor", lambda E_, hch=hch, tt4=tt4,
                               piece=piece: E_.matmul(
                            out=pb[4 + piece % 2][:, 0:512],
                            lhsT=hs_sb[:, hch * 512 + tt4 * 128:
                                       hch * 512 + (tt4 + 1) * 128],
                            rhs=wsdn[:, hch * D + piece * 512:
                                     hch * D + (piece + 1) * 512],
                            start=(hch == 0), stop=(hch == 3)))
                    d_mm = p.mark("tensor")
                    p.wait("vector", d_mm)
                    p.wait("vector", x_ev)
                    ev = p.emit("vector", lambda E_, piece=piece:
                                E_.tensor_tensor(
                        acc[:, piece * 512:(piece + 1) * 512],
                        pb[4 + piece % 2][:, 0:512],
                        x1t[:, piece * 512:(piece + 1) * 512], op=ALU.add),
                        inc="ev")
                    p.wait("tensor", ev)
                a_ev = p.mark("vector")
                p.wait("gpsimd", a_ev)
                p.dma("gpsimd", hb["acc"][t * 128:(t + 1) * 128, :], acc[:, :])
                p.barrier()
        p.flush()

    if C_SKIP_EXPERTS:
        emit_ln_pass(p, hb["acc"], hb["ln2tok"], hb["ln2fm"])
        return
    with ExitStack() as st:
        sb = lambda n, s, dt=F32: st.enter_context(nc.sbuf_tensor(n, s, dt))
        I32 = mybir.dt.int32
        wup = sb("c5_wup", [128, 16 * 2 * EH])
        wdn = sb("c5_wdn", [128, 2 * D])
        xe = sb("c5_xe", [128, 2 * D])
        xefm = sb("c5_xefm", [128, 16 * 2 * 128])
        hr = sb("c5_hr", [128, 2 * EH])
        sil = sb("c5_sil", [128, EH])
        wrow = sb("c5_wrow", [1, PEXP])
        wrep = sb("c5_wrep", [128, PEXP])
        ydown = sb("c5_ydown", [128, 2 * D])
        lcol = sb("c5_lcol", [128, 2])
        lcoli = sb("c5_lcoli", [128, 1], I32)
        ones_sb = sb("c5_ones", [1, 128])

        p.emit("vector", lambda E_: E_.memset(ones_sb[:, :], 1.0))
        rbc2 = nc.alloc_register(mybir.EngineType.Pool, "c5_bc")
        p.emit("gpsimd", lambda E_: nc.reg_mov(rbc2, NTOK - 1))
        p.barrier()
        from concourse.bass_core_rust import IndirectOffsetOnAxis
        for e in range(E):
            for ch in range(16):
                p.dma("sync", wup[:, ch * 512:(ch + 1) * 512],
                      io["w_up"][e, ch * 128:(ch + 1) * 128, :])
            for ch in range(2):
                p.dma("sync", wdn[:, ch * D:(ch + 1) * D],
                      io["w_dn"][e, ch * 128:(ch + 1) * 128, :])
            xe_ev = p.dma("sync", xe[:, :].rearrange("p (r c) -> p r c", r=2),
                          hb["xdisp"][e * PEXP:(e + 1) * PEXP, :]
                              .rearrange("(r p) c -> p r c", p=128))
            w_ev = p.dma("sync", wrow[0:1, :],
                         hb["wlist"][e * PEXP:(e + 1) * PEXP, :]
                             .rearrange("n o -> o n"))
            fm_ev = transposes(
                p, [(xe[:, rc * D + dch * 128:rc * D + (dch + 1) * 128],
                     xefm[:, dch * 256 + rc * 128:dch * 256 + (rc + 1) * 128],
                     128) for rc in range(2) for dch in range(16)],
                pre_ev=xe_ev, banks=(0, 1))
            p.wait("tensor", fm_ev)
            for hch in range(4):
                for dch in range(16):
                    p.emit("tensor", lambda E_, hch=hch, dch=dch: E_.matmul(
                        out=pb[2 + hch][:, 0:256],
                        lhsT=wup[:, dch * 512 + hch * 128:
                                 dch * 512 + (hch + 1) * 128],
                        rhs=xefm[:, dch * 256:(dch + 1) * 256],
                        start=(dch == 0), stop=(dch == 15)))
            up_mm = p.mark("tensor")
            wr_mm = p.emit("tensor", lambda E_: E_.matmul(
                out=pb[6][:, 0:PEXP], lhsT=ones_sb[0:1, 0:128],
                rhs=wrow[0:1, :], start=True, stop=True), inc="mm")
            p.wait("scalar", up_mm)
            s0 = p.emit("scalar", lambda E_: E_.activation(
                sil[:, :], pb[2][:, 0:256], AF.Silu), inc="ac")
            p.wait("vector", s0)
            h0 = p.emit("vector", lambda E_: E_.tensor_tensor(
                hr[:, 0:256], sil[:, :], pb[4][:, 0:256], op=ALU.mult),
                inc="ev")
            p.wait("scalar", h0)
            s1 = p.emit("scalar", lambda E_: E_.activation(
                sil[:, :], pb[3][:, 0:256], AF.Silu), inc="ac")
            p.wait("vector", s1)
            p.emit("vector", lambda E_: E_.tensor_tensor(
                hr[:, 256:512], sil[:, :], pb[5][:, 0:256], op=ALU.mult))
            p.wait("vector", wr_mm)
            p.emit("vector", lambda E_: E_.tensor_copy(wrep[:, :],
                                                       pb[6][:, 0:PEXP]))
            p.chain("vector")
            p.emit("vector", lambda E_: E_.tensor_tensor(
                hr[:, 0:256], hr[:, 0:256], wrep[:, :], op=ALU.mult))
            hr_ev = p.emit("vector", lambda E_: E_.tensor_tensor(
                hr[:, 256:512], hr[:, 256:512], wrep[:, :], op=ALU.mult),
                inc="ev")
            p.wait("tensor", hr_ev)
            for sc in range(2):
                for piece in range(4):
                    for hch in range(2):
                        p.emit("tensor", lambda E_, sc=sc, piece=piece,
                               hch=hch: E_.matmul(
                            out=pb[2 + piece % 2][:, 0:512],
                            lhsT=hr[:, hch * 256 + sc * 128:
                                    hch * 256 + (sc + 1) * 128],
                            rhs=wdn[:, hch * D + piece * 512:
                                    hch * D + (piece + 1) * 512],
                            start=(hch == 0), stop=(hch == 1)))
                    d_mm = p.mark("tensor")
                    p.wait("vector", d_mm)
                    ev = p.emit("vector", lambda E_, sc=sc, piece=piece:
                                E_.tensor_copy(
                        ydown[:, sc * D + piece * 512:sc * D + (piece + 1) * 512],
                        pb[2 + piece % 2][:, 0:512]), inc="ev")
                    p.wait("tensor", ev)
            y_ev = p.mark("vector")
            lc_ev = p.dma("sync", lcol[:, 0:1],
                          hb["lists"][e * PEXP:e * PEXP + 128, :])
            lc2_ev = p.dma("sync", lcol[:, 1:2],
                           hb["lists"][e * PEXP + 128:(e + 1) * PEXP, :])
            for sc in range(2):
                p.wait("vector", lc2_ev)
                ci = p.emit("vector", lambda E_, sc=sc: E_.tensor_copy(
                    lcoli[:, :], lcol[:, sc:sc + 1]), inc="ev")
                p.wait("gpsimd", ci)
                p.wait("gpsimd", y_ev)
                p.emit("gpsimd", lambda E_, sc=sc: E_.indirect_dma_start(
                    hb["acc"][:, :], IndirectOffsetOnAxis(lcoli[:, 0:1], 0),
                    ydown[:, sc * D:(sc + 1) * D], None,
                    bounds_check=rbc2, oob_is_err=False,
                    compute_op=ALU.add), inc="st", inc_n=16)
                p.wait_all("gpsimd", "st")
                p.wait_all("vector", "st")
            p.barrier()
        p.flush()

    emit_ln_pass(p, hb["acc"], hb["ln2tok"], hb["ln2fm"])


def emit_ln_pass(p, src_hbm, out_lntok, out_lnfm):
    nc, pb = p.nc, p.pb
    with ExitStack() as st:
        sb = lambda n, s, dt=F32: st.enter_context(nc.sbuf_tensor(n, s, dt))
        x_sb = sb("lp_x", [128, D])
        sq = sb("lp_sq", [128, D])
        stats = sb("lp_stats", [128, 16])
        lnfm = sb("lp_lnfm", [128, 16 * 128])
        for t in range(NT):
            x_ev = p.dma("sync", x_sb[:, :], src_hbm[t * 128:(t + 1) * 128, :])
            ln_ev = ln_tile(p, x_sb[:, :], sq, stats, x_ev)
            p.wait("gpsimd", ln_ev)
            p.dma("gpsimd", out_lntok[t * 128:(t + 1) * 128, :], x_sb[:, :])
            fm_ev = transposes(
                p, [(x_sb[:, d * 128:(d + 1) * 128],
                     lnfm[:, d * 128:(d + 1) * 128], 128) for d in range(16)],
                pre_ev=ln_ev)
            p.wait("gpsimd", fm_ev)
            p.dma("gpsimd",
                  out_lnfm[:, :, t * 128:(t + 1) * 128]
                      .rearrange("d p c -> p d c"),
                  lnfm[:, :].rearrange("p (d c) -> p d c", d=16))
            p.barrier()
        p.flush()



_FMPROJ_N = [0]


def emit_fm_proj(p, lnfm_hbm, w_io, OC, out_hbm, bias_col_io=None):
    nc, pb = p.nc, p.pb
    _FMPROJ_N[0] += 1
    tag = f"fp{_FMPROJ_N[0]}"
    with ExitStack() as st:
        sb = lambda n, s, dt=F32: st.enter_context(nc.sbuf_tensor(tag + n, s, dt))
        lnfm_sb = sb("_ln", [128, 16 * 2048])
        wslab = sb("_w", [128, 16 * 128])
        o_sb = sb("_o", [128, 2 * 512])
        bcol = sb("_b", [128, OC // 128]) if bias_col_io is not None else None
        for ch in range(16):
            p.dma("sync", lnfm_sb[:, ch * 2048:(ch + 1) * 2048],
                  lnfm_hbm[ch, :, :])
        if bias_col_io is not None:
            p.dma("sync", bcol[:, :], bias_col_io[:, :])
        p.barrier()
        for och in range(OC // 128):
            w_ev = p.dma("sync",
                         wslab[:, :].rearrange("p (d c) -> p d c", d=16),
                         w_io[:, och * 128:(och + 1) * 128]
                             .rearrange("(d p) c -> p d c", p=128))
            p.wait("tensor", w_ev)
            for panel in range(4):
                bk = panel % 2
                for dch in range(16):
                    p.emit("tensor", lambda E_, dch=dch, panel=panel, bk=bk:
                           E_.matmul(
                        out=pb[2 + bk][:, 0:512],
                        lhsT=wslab[:, dch * 128:(dch + 1) * 128],
                        rhs=lnfm_sb[:, dch * 2048 + panel * 512:
                                    dch * 2048 + (panel + 1) * 512],
                        start=(dch == 0), stop=(dch == 15)))
                mm_ev = p.mark("tensor")
                if bias_col_io is not None:
                    p.wait("scalar", mm_ev)
                    ev = p.emit("scalar", lambda E_, bk=bk, och=och:
                                E_.activation(
                        o_sb[:, bk * 512:(bk + 1) * 512], pb[2 + bk][:, 0:512],
                        AF.Identity, bias=bcol[:, och:och + 1]), inc="ac")
                else:
                    p.wait("vector", mm_ev)
                    ev = p.emit("vector", lambda E_, bk=bk: E_.tensor_copy(
                        o_sb[:, bk * 512:(bk + 1) * 512], pb[2 + bk][:, 0:512]),
                        inc="ev")
                p.wait("gpsimd", ev)
                st_ev = p.dma("gpsimd",
                              out_hbm[och * 128:(och + 1) * 128,
                                      panel * 512:(panel + 1) * 512],
                              o_sb[:, bk * 512:(bk + 1) * 512])
                p.wait("tensor", ev)
                if bias_col_io is not None:
                    p.wait("scalar", st_ev)
                else:
                    p.wait("vector", st_ev)
            p.barrier()
        p.flush()


def emit_phase_d(p, io, hb):
    nc, pb = p.nc, p.pb
    emit_fm_proj(p, hb["ln2fm"], io["w_qkv1T"], 3 * D, hb["qkv1"],
                 bias_col_io=io["b1col"])

    with ExitStack() as st:
        sb = lambda n, s, dt=F32: st.enter_context(nc.sbuf_tensor(n, s, dt))
        a_sb = sb("d2_a", [128, 16 * 128])
        vtok = sb("d2_v", [128, 16 * 128])
        for t in range(NT):
            a_ev = p.dma("sync", a_sb[:, :].rearrange("p (d c) -> p d c", d=16),
                         hb["qkv1"][2 * D:3 * D, t * 128:(t + 1) * 128]
                             .rearrange("(d p) c -> p d c", p=128))
            fm_ev = transposes(
                p, [(a_sb[:, ch * 128:(ch + 1) * 128],
                     vtok[:, ch * 128:(ch + 1) * 128], 128) for ch in range(16)],
                pre_ev=a_ev)
            p.wait("gpsimd", fm_ev)
            p.dma("gpsimd", hb["v1_in"][t * 128:(t + 1) * 128, :], vtok[:, :])
            p.barrier()
        p.flush()
    p.wait_all("gpsimd", "st")
    cc1 = p.emit("gpsimd", lambda E_: E_.collective_compute(
        "AllGather", ALU.bypass,
        replica_groups=[[0, 1, 2, 3], [4, 5, 6, 7]],
        ins=[hb["qkv1"][D:2 * D, :].opt()],
        outs=[hb["k1"].rearrange("r d t -> (r d) t").opt()]), inc="cc")
    cc2 = p.emit("gpsimd", lambda E_: E_.collective_compute(
        "AllGather", ALU.bypass,
        replica_groups=[[0, 1, 2, 3], [4, 5, 6, 7]],
        ins=[hb["v1_in"][:, :].opt()],
        outs=[hb["v1"].rearrange("r t d -> (r t) d").opt()]), inc="cc")
    p.wait("gpsimd", cc2)
    p.barrier()
    p.flush()

    with ExitStack() as st:
        sb = lambda n, s, dt=F32: st.enter_context(nc.sbuf_tensor(n, s, dt))
        kp = sb("d3_kp", [128, 4 * 16 * 128])
        vp = sb("d3_vp", [128, 4 * 2052])
        q_sb = sb("d3_q", [128, 4 * 16 * 128])
        out_sb = sb("d3_out", [128, 4 * 2052])
        probs = sb("d3_probs", [128, 2 * 128])
        ctx = sb("d3_ctx", [128, 2048])
        ctxT = sb("d3_ctxT", [128, 16 * 128])
        rr = sb("d3_rr", [128, 4])

        ones_ap = vp[:, :].rearrange("p (t c) -> p t c", t=4)[:, :, 2048:2049]
        p.emit("vector", lambda E_: E_.memset(ones_ap, 1.0))
        p.barrier()
        for qgrp in range(4):
            for qs in range(4):
                p.dma("sync",
                      q_sb[:, qs * 2048:(qs + 1) * 2048]
                          .rearrange("p (d c) -> p d c", d=16),
                      hb["qkv1"][0:D, (qgrp * 4 + qs) * 128:
                                 (qgrp * 4 + qs + 1) * 128]
                          .rearrange("(d p) c -> p d c", p=128))
            z_ev = p.emit("vector", lambda E_: E_.memset(out_sb[:, :], 0.0),
                          inc="ev")
            p.barrier()
            for kpan in range(16):
                for ki in range(4):
                    kt = kpan * 4 + ki
                    r, lt = kt // 16, kt % 16
                    p.dma("sync",
                          kp[:, ki * 2048:(ki + 1) * 2048]
                              .rearrange("p (d c) -> p d c", d=16),
                          hb["k1"][r, :, lt * 128:(lt + 1) * 128]
                              .rearrange("(d p) c -> p d c", p=128))
                    p.dma("sync", vp[:, ki * 2052:ki * 2052 + 2048],
                          hb["v1"][r, lt * 128:(lt + 1) * 128, :])
                p.barrier()
                for qs in range(4):
                    ex_ev = [None, None]
                    av_done = [None, None]
                    for ki in range(4):
                        par = ki % 2
                        p.wait("tensor", ex_ev[par])
                        for dch in range(16):
                            p.emit("tensor", lambda E_, ki=ki, dch=dch, qs=qs,
                                   par=par: E_.matmul(
                                out=pb[par][:, 0:128],
                                lhsT=kp[:, ki * 2048 + dch * 128:
                                        ki * 2048 + (dch + 1) * 128],
                                rhs=q_sb[:, qs * 2048 + dch * 128:
                                         qs * 2048 + (dch + 1) * 128],
                                start=(dch == 0), stop=(dch == 15)))
                        sc_ev = p.mark("tensor")
                        p.wait("scalar", sc_ev)
                        p.wait("scalar", av_done[par])
                        ex_ev[par] = p.emit("scalar", lambda E_, par=par:
                                            E_.activation(
                            probs[:, par * 128:(par + 1) * 128],
                            pb[par][:, 0:128], AF.Exp, scale=SCALE1A,
                            bias=-SHIFT), inc="ac")
                        p.wait("tensor", ex_ev[par])
                        for piece in range(4):
                            p.emit("tensor", lambda E_, ki=ki, par=par,
                                   piece=piece: E_.matmul(
                                out=pb[2 + piece][:, 0:512],
                                lhsT=probs[:, par * 128:(par + 1) * 128],
                                rhs=vp[:, ki * 2052 + piece * 512:
                                       ki * 2052 + (piece + 1) * 512],
                                start=(ki == 0), stop=(ki == 3)))
                        p.emit("tensor", lambda E_, ki=ki, par=par: E_.matmul(
                            out=pb[6][:, 0:1],
                            lhsT=probs[:, par * 128:(par + 1) * 128],
                            rhs=vp[:, ki * 2052 + 2048:ki * 2052 + 2049],
                            start=(ki == 0), stop=(ki == 3)))
                        av_done[par] = p.mark("tensor")
                    p.wait("vector", av_done[3 % 2])
                    p.wait("vector", z_ev)
                    for piece in range(4):
                        p.emit("vector", lambda E_, qs=qs, piece=piece:
                               E_.scalar_tensor_tensor(
                            out_sb[:, qs * 2052 + piece * 512:
                                   qs * 2052 + (piece + 1) * 512],
                            pb[2 + piece][:, 0:512], 1.0,
                            out_sb[:, qs * 2052 + piece * 512:
                                   qs * 2052 + (piece + 1) * 512],
                            op0=ALU.mult, op1=ALU.add))
                    ev = p.emit("vector", lambda E_, qs=qs:
                                E_.scalar_tensor_tensor(
                        out_sb[:, qs * 2052 + 2048:qs * 2052 + 2049],
                        pb[6][:, 0:1], 1.0,
                        out_sb[:, qs * 2052 + 2048:qs * 2052 + 2049],
                        op0=ALU.mult, op1=ALU.add), inc="ev")
                    p.barrier()
            for qs in range(4):
                p.emit("vector", lambda E_, qs=qs: E_.reciprocal(
                    rr[:, qs:qs + 1], out_sb[:, qs * 2052 + 2048:
                                             qs * 2052 + 2049]))
                p.chain("vector")
                n_ev = p.emit("vector", lambda E_, qs=qs: E_.tensor_scalar(
                    ctx[:, :], out_sb[:, qs * 2052:qs * 2052 + 2048],
                    rr[:, qs:qs + 1], None, op0=ALU.mult), inc="ev")
                cT_ev = transposes(
                    p, [(ctx[:, ch * 128:(ch + 1) * 128],
                         ctxT[:, ch * 128:(ch + 1) * 128], 128)
                        for ch in range(16)], pre_ev=n_ev, banks=(0, 1))
                p.wait("gpsimd", cT_ev)
                p.dma("gpsimd",
                      hb["ctxT"][:, :, (qgrp * 4 + qs) * 128:
                                 (qgrp * 4 + qs + 1) * 128]
                          .rearrange("h v c -> v h c"),
                      ctxT[:, :].rearrange("v (h c) -> v h c", h=16))
                p.barrier()
        p.flush()

    emit_proj_res_ln(p, io, hb, att_src=hb["ctxT"], w_in=io["w_out1T"],
                     w_cols=D, res_src=hb["acc"], out_x=hb["x3"],
                     out_lntok=hb["ln3tok"], out_lnfm=hb["ln3fm"],
                     bias_io=io["b_out1_rep"])



E_STOP = "full"


def emit_phase_e(p, io, hb, y_out):
    nc, pb = p.nc, p.pb
    from concourse.bass import AP as _AP
    emit_fm_proj(p, hb["ln3fm"], io["w_qkv2T"], 3 * D, hb["qkv2"])
    if E_STOP == "proj":
        return

    with ExitStack() as st:
        sb = lambda n, s, dt=F32: st.enter_context(nc.sbuf_tensor(n, s, dt))
        a_sb = sb("e1_a", [128, 16 * 128])
        w2g = sb("e1_w2g", [128, 16 * NH1])
        gat = sb("e1_gat", [128, NH1])
        eq = sb("e1_eq", [128, NH1])
        stats = sb("e1_stats", [128, 8])
        kv_sb = sb("e1_kv", [128, 16 * 128])
        ktok = sb("e1_ktok", [128, 2048])
        vtok = sb("e1_vtok", [128, 2048])
        tmp = sb("e1_tmp", [128, 2048])
        ksh = sb("e1_ksh", [128, 128])
        vsh = sb("e1_vsh", [128, 128])
        kshT = sb("e1_kshT", [128, 128])

        for ch in range(16):
            p.dma("sync", w2g[:, ch * NH1:(ch + 1) * NH1],
                  io["w_gate2T"][ch * 128:(ch + 1) * 128, :])
        p.barrier()
        for t in range(NT):
            a_ev = p.dma("sync", a_sb[:, :].rearrange("p (d c) -> p d c", d=16),
                         hb["ln3fm"][:, :, t * 128:(t + 1) * 128]
                             .rearrange("d p c -> p d c"))
            p.wait("tensor", a_ev)
            for ch in range(16):
                p.emit("tensor", lambda E_, ch=ch: E_.matmul(
                    out=pb[6][:, 0:NH1], lhsT=a_sb[:, ch * 128:(ch + 1) * 128],
                    rhs=w2g[:, ch * NH1:(ch + 1) * NH1],
                    start=(ch == 0), stop=(ch == 15)))
            g_mm = p.mark("tensor")
            p.wait("vector", g_mm)
            p.emit("vector", lambda E_: E_.reduce_max(
                stats[:, 0:1], pb[6][:, 0:NH1], axis=AX.X, negate=True))
            nm_ev = p.mark("vector")
            p.wait("scalar", nm_ev)
            ex_ev = p.emit("scalar", lambda E_: E_.activation(
                gat[:, :], pb[6][:, 0:NH1], AF.Exp, bias=stats[:, 0:1],
                accum_out=stats[:, 1:2]), inc="ac")
            p.wait("vector", ex_ev)
            p.emit("vector", lambda E_: E_.reciprocal(stats[:, 2:3],
                                                      stats[:, 1:2]))
            p.chain("vector")
            p.emit("vector", lambda E_: E_.tensor_scalar(
                gat[:, :], gat[:, :], stats[:, 2:3], None, op0=ALU.mult))
            p.chain("vector")
            p.emit("vector", lambda E_: E_.reduce_max(
                stats[:, 3:4], gat[:, :], axis=AX.X))
            p.chain("vector")
            eq_ev = p.emit("vector", lambda E_: E_.tensor_scalar(
                eq[:, :], gat[:, :], stats[:, 3:4], None, op0=ALU.is_equal),
                inc="ev")
            p.barrier()
            kv_evs = []
            for bi, (blk, dstt) in enumerate(((1, ktok), (2, vtok))):
                if bi == 1:
                    p.wait("sync", kv_evs[0])
                b_ev = p.dma("sync",
                             kv_sb[:, :].rearrange("p (d c) -> p d c", d=16),
                             hb["qkv2"][blk * D:(blk + 1) * D,
                                        t * 128:(t + 1) * 128]
                                 .rearrange("(d p) c -> p d c", p=128))
                fm_ev = transposes(
                    p, [(kv_sb[:, ch * 128:(ch + 1) * 128],
                         dstt[:, ch * 128:(ch + 1) * 128], 128)
                        for ch in range(16)], pre_ev=b_ev,
                    banks=(0, 1) if bi == 0 else (2, 3))
                kv_evs.append(fm_ev)
            eq_b = _AP(tensor=eq[:, 0:1].tensor, offset=eq[:, 0:1].offset,
                       ap=[list(eq[:, 0:1].ap[0]), [1, NH1], [0, 128]])
            p.barrier()
            p.wait("vector", kv_evs[0])
            p.emit("vector", lambda E_: E_.tensor_tensor(
                tmp[:, :].rearrange("p (h c) -> p h c", h=NH1),
                ktok[:, :].rearrange("p (h c) -> p h c", h=NH1),
                eq_b, op=ALU.mult))
            p.chain("vector")
            k_ev = p.emit("vector", lambda E_: E_.reduce_sum(
                ksh[:, :],
                _AP(tensor=tmp[:, 0:1].tensor, offset=tmp[:, 0:1].offset,
                    ap=[list(tmp[:, 0:1].ap[0]), [1, 128], [128, NH1]]),
                axis=AX.X), inc="ev")
            p.wait("vector", kv_evs[1])
            p.emit("vector", lambda E_: E_.tensor_tensor(
                tmp[:, :].rearrange("p (h c) -> p h c", h=NH1),
                vtok[:, :].rearrange("p (h c) -> p h c", h=NH1),
                eq_b, op=ALU.mult))
            p.chain("vector")
            p.emit("vector", lambda E_: E_.reduce_sum(
                vsh[:, :],
                _AP(tensor=tmp[:, 0:1].tensor, offset=tmp[:, 0:1].offset,
                    ap=[list(tmp[:, 0:1].ap[0]), [1, 128], [128, NH1]]),
                axis=AX.X))
            p.chain("vector")
            v_ev = p.emit("vector", lambda E_: E_.tensor_scalar(
                vsh[:, :], vsh[:, :], stats[:, 3:4], None, op0=ALU.mult),
                inc="ev")
            p.barrier()
            kT_ev = transposes(p, [(ksh[:, :], kshT[:, :], 128)],
                               pre_ev=k_ev, banks=(4, 5))
            p.wait("gpsimd", v_ev)
            p.dma("gpsimd", hb["vsh_in"][t * 128:(t + 1) * 128, :], vsh[:, :])
            p.wait("gpsimd", kT_ev)
            p.dma("gpsimd", hb["ksh_in"][:, t * 128:(t + 1) * 128],
                  kshT[:, :])
            p.barrier()
        p.flush()
    if E_STOP == "gat":
        return
    p.wait_all("gpsimd", "st")
    cc1 = p.emit("gpsimd", lambda E_: E_.collective_compute(
        "AllGather", ALU.bypass,
        replica_groups=[[0, 1, 2, 3], [4, 5, 6, 7]],
        ins=[hb["ksh_in"][:, :].opt()],
        outs=[hb["ksh"].rearrange("r d t -> (r d) t").opt()]), inc="cc")
    cc2 = p.emit("gpsimd", lambda E_: E_.collective_compute(
        "AllGather", ALU.bypass,
        replica_groups=[[0, 1, 2, 3], [4, 5, 6, 7]],
        ins=[hb["vsh_in"][:, :].opt()],
        outs=[hb["vsh"].rearrange("r t d -> (r t) d").opt()]), inc="cc")
    p.wait("gpsimd", cc2)
    p.barrier()
    p.flush()
    if E_STOP == "sel":
        return

    with ExitStack() as st:
        sb = lambda n, s, dt=F32: st.enter_context(nc.sbuf_tensor(n, s, dt))
        kf = sb("e2_kf", [128, 4 * 2048])
        vf = sb("e2_vf", [128, 64 * 129])
        kid = sb("e2_kid", [128, 64])
        qid = sb("e2_qid", [128, 16 * 128])
        qj = sb("e2_qj", [128, 16 * 128])
        x3t = sb("e2_x3", [128, 2048])
        y_sb = sb("e2_y", [128, 2048])
        probs = sb("e2_probs", [128, 2 * 128])
        msk = sb("e2_msk", [128, 25 * 128])
        rr = sb("e2_rr", [128, 4])

        for r in range(4):
            p.dma("sync", kf[:, r * 2048:(r + 1) * 2048],
                  hb["ksh"][r, :, :])
            p.dma("sync",
                  vf[:, :].rearrange("p (t c) -> p t c", t=64)
                      [:, r * 16:(r + 1) * 16, 0:128],
                  hb["vsh"][r, :, :].rearrange("(t q) c -> q t c", q=128))
        p.dma("sync", kid[:, :], io["kid"][:, :])
        p.dma("sync", qid[:, :], io["qid_rep"][:, :])
        ones_ap = vf[:, :].rearrange("p (t c) -> p t c", t=64)[:, :, 128:129]
        p.emit("vector", lambda E_: E_.memset(ones_ap, 1.0))
        p.barrier()

        for j in range(NT):
            gmin = j if j < 8 else 32 + (j - 8)
            gmax = 24 + j if j < 8 else 56 + (j - 8)
            q_ev = p.dma("sync", qj[:, :].rearrange("p (d c) -> p d c", d=16),
                         hb["qkv2"][0:D, j * 128:(j + 1) * 128]
                             .rearrange("(d p) c -> p d c", p=128))
            x_ev = p.dma("sync", x3t[:, :], hb["x3"][j * 128:(j + 1) * 128, :])
            mevs = None
            for kt in range(gmin, gmax + 1):
                mevs = p.emit("vector", lambda E_, kt=kt, j=j, gmin=gmin:
                              E_.tensor_scalar(
                    msk[:, (kt - gmin) * 128:(kt - gmin + 1) * 128],
                    qid[:, j * 128:(j + 1) * 128], kid[:, kt:kt + 1], None,
                    op0=ALU.is_ge), inc="ev")
            for h in range(NH1):
                ex_ev = [None, None]
                mk_ev = [None, None]
                av_done = [None, None]
                for kt in range(gmax + 1):
                    par = kt % 2
                    p.wait("tensor", mk_ev[par])
                    p.wait("tensor", ex_ev[par])
                    sc = p.emit("tensor", lambda E_, kt=kt, h=h, par=par:
                                E_.matmul(
                        out=pb[par][:, 0:128],
                        lhsT=kf[:, kt * 128:(kt + 1) * 128],
                        rhs=qj[:, h * 128:(h + 1) * 128],
                        start=True, stop=True), inc="mm")
                    p.wait("scalar", sc)
                    p.wait("scalar", av_done[par])
                    p.wait("scalar", mk_ev[par])
                    ex_ev[par] = p.emit("scalar", lambda E_, par=par:
                                        E_.activation(
                        probs[:, par * 128:(par + 1) * 128],
                        pb[par][:, 0:128], AF.Exp, scale=SCALE1F,
                        bias=-SHIFT), inc="ac")
                    if kt >= gmin:
                        p.wait("vector", ex_ev[par])
                        p.wait("vector", mevs)
                        mk_ev[par] = p.emit("vector", lambda E_, par=par,
                                            kt=kt, gmin=gmin: E_.tensor_tensor(
                            probs[:, par * 128:(par + 1) * 128],
                            probs[:, par * 128:(par + 1) * 128],
                            msk[:, (kt - gmin) * 128:(kt - gmin + 1) * 128],
                            op=ALU.mult), inc="ev")
                        p.wait("tensor", mk_ev[par])
                    else:
                        mk_ev[par] = None
                        p.wait("tensor", ex_ev[par])
                    p.emit("tensor", lambda E_, kt=kt, par=par, gmax=gmax:
                           E_.matmul(
                        out=pb[2][:, 0:129],
                        lhsT=probs[:, par * 128:(par + 1) * 128],
                        rhs=vf[:, kt * 129:(kt + 1) * 129],
                        start=(kt == 0), stop=(kt == gmax)))
                    av_done[par] = p.mark("tensor")
                p.wait("vector", av_done[gmax % 2])
                p.emit("vector", lambda E_: E_.reciprocal(rr[:, 0:1],
                                                          pb[2][:, 128:129]))
                p.chain("vector")
                n_ev = p.emit("vector", lambda E_, h=h: E_.tensor_scalar(
                    y_sb[:, h * 128:(h + 1) * 128], pb[2][:, 0:128],
                    rr[:, 0:1], None, op0=ALU.mult), inc="ev")
                p.wait("tensor", n_ev)
            p.wait("vector", x_ev)
            x4_ev = p.emit("vector", lambda E_: E_.tensor_tensor(
                y_sb[:, :], y_sb[:, :], x3t[:, :], op=ALU.add), inc="ev")
            p.wait("gpsimd", x4_ev)
            p.dma("gpsimd", y_out[j * 128:(j + 1) * 128, :], y_sb[:, :])
            p.barrier()
        p.wait_all("gpsimd", "st")
        p.barrier()
        p.flush()


def build_program(debug_taps=(),
                  phases=("a1", "a2", "ag0", "b", "b2", "c", "d", "e")):
    nc = bass.Bass("TRN2", target_bir_lowering=False, detect_race_conditions=False)

    def ein(name, shape, dtype=F32):
        return nc.dram_tensor(name, list(shape), dtype,
                              kind="ExternalInput").ap()

    io = {}
    io["x0"] = ein("x0", (NTOK, D))
    io["wq_a"] = ein("wq_a", (D, QR))
    io["gq_rep"] = ein("gq_rep", (128, QR))
    io["wq_b"] = ein("wq_b", (QR, H0 * DK))
    io["wkv_a"] = ein("wkv_a", (D, KVR + ROPE))
    io["gkv_rep"] = ein("gkv_rep", (128, KVR))
    io["wkv_b"] = ein("wkv_b", (KVR, H0 * (NOPE + VD)))
    io["wo"] = ein("wo", (H0 * VD, D))
    io["w_gate"] = ein("w_gate", (D, E))
    io["w_sup"] = ein("w_sup", (D, 2 * HS))
    io["w_sdn"] = ein("w_sdn", (HS, D))
    io["w_up"] = ein("w_up", (E, D, 2 * EH))
    io["w_dn"] = ein("w_dn", (E, EH, D))
    io["w_qkv1T"] = ein("w_qkv1T", (D, 3 * D))
    io["b_qkv1_rep"] = ein("b_qkv1_rep", (128, 3 * D))
    io["w_out1T"] = ein("w_out1T", (D, D))
    io["b_out1_rep"] = ein("b_out1_rep", (128, D))
    io["w_gate2T"] = ein("w_gate2T", (D, NH1))
    io["w_qkv2T"] = ein("w_qkv2T", (D, 3 * D))
    io["cosT"] = ein("cosT", (NTOK, 32))
    io["sinT"] = ein("sinT", (NTOK, 32))
    io["carry_maskT"] = ein("carry_maskT", (16, 2))
    io["pcol"] = ein("pcol", (128, 1))
    io["b1col"] = ein("b1col", (128, 48))
    io["kid"] = ein("kid", (128, 64))
    io["qid_rep"] = ein("qid_rep", (128, 16 * 128))
    io["identity"] = ein("identity", (128, 128))
    io["ut_strict"] = ein("ut_strict", (128, 128))
    io["iota256_rep"] = ein("iota256_rep", (128, 256))
    io["iota64_rep"] = ein("iota64_rep", (128, 64))

    y = nc.dram_tensor("y", [NTOK, D], F32, kind="ExternalOutput").ap()

    def scr(name, shape, dtype=F32):
        return nc.dram_tensor(name, list(shape), dtype, kind="Internal").ap()

    hb = {}
    hb["qlat"] = scr("s_qlat", (NTOK, QR))
    hb["qtok"] = scr("s_qtok", (NTOK, H0 * DK))
    hb["cnorm"] = scr("s_cnorm", (NTOK, KVR))
    hb["kpeT"] = scr("s_kpeT", (ROPE, NTOK))
    hb["qfm"] = scr("s_qfm", (H0, DK, NTOK))
    hb["kfm_in"] = scr("s_kfm_in", (H0 * DK, NTOK))
    hb["v_in"] = scr("s_v_in", (NTOK, H0 * VD))
    hb["kfm"] = scr("s_kfm", (4, H0 * DK, NTOK))
    hb["v"] = scr("s_v", (4, NTOK, H0 * VD))
    hb["attnT"] = scr("s_attnT", (H0, VD, NTOK))
    hb["x1"] = scr("s_x1", (NTOK, D))
    hb["ln1tok"] = scr("s_ln1tok", (NTOK, D))
    hb["ln1fm"] = scr("s_ln1fm", (16, 128, NTOK))
    hb["xdisp"] = scr("s_xdisp", (E * PEXP, D))
    hb["lists"] = scr("s_lists", (E * PEXP, 1))
    hb["wlist"] = scr("s_wlist", (E * PEXP, 1))
    hb["cnt_in"] = scr("s_cnt_in", (2, E))
    hb["cnt_all"] = scr("s_cnt_all", (8, 2, E))
    hb["acc"] = scr("s_acc", (NTOK, D))
    hb["ln2tok"] = scr("s_ln2tok", (NTOK, D))
    hb["ln2fm"] = scr("s_ln2fm", (16, 128, NTOK))
    hb["qkv1"] = scr("s_qkv1", (3 * D, NTOK))
    hb["v1_in"] = scr("s_v1_in", (NTOK, D))
    hb["k1"] = scr("s_k1", (4, D, NTOK))
    hb["v1"] = scr("s_v1", (4, NTOK, D))
    hb["ctxT"] = scr("s_ctxT", (16, 128, NTOK))
    hb["x3"] = scr("s_x3", (NTOK, D))
    hb["ln3tok"] = scr("s_ln3tok", (NTOK, D))
    hb["ln3fm"] = scr("s_ln3fm", (16, 128, NTOK))
    hb["qkv2"] = scr("s_qkv2", (3 * D, NTOK))
    hb["ksh_in"] = scr("s_ksh_in", (HD1, NTOK))
    hb["vsh_in"] = scr("s_vsh_in", (NTOK, HD1))
    hb["ksh"] = scr("s_ksh", (4, HD1, NTOK))
    hb["vsh"] = scr("s_vsh", (4, NTOK, HD1))

    with ExitStack() as stack:
        p = P(nc, stack)
        p.ident = p.sb("ident_sb", [128, 128])
        p.dma("sync", p.ident[:, :], io["identity"][:, :])
        consts = [EPS_LN, EPS_RMS, 1.0 / QR, 1.0 / KVR, 1.0 / D, SCALE0,
                  SCALE1A, SCALE1F, -SHIFT, 1.0 / HS, 1.0 / EH, 0.5, 2.0]
        cpool = p.sb("const_pool", [128, len(consts)])
        for k, v in enumerate(consts):
            p.emit("vector", lambda E, k=k, v=v: E.memset(cpool[:, k:k + 1], v))
            nc.const_aps.register(F32, float(v), cpool[:, k:k + 1])
        p.barrier()
        p.flush()

        if "a1" in phases:
            emit_phase_a1(p, io, hb)
        if "a2" in phases:
            emit_phase_a2(p, io, hb)
        if "ag0" in phases:
            emit_ag_l0(p, hb)
        if "b" in phases:
            emit_phase_b(p, io, hb)
        if "b2" in phases:
            emit_phase_b2(p, io, hb)
        if "c" in phases:
            emit_phase_c(p, io, hb)
        if "d" in phases:
            emit_phase_d(p, io, hb)
        if "e" in phases:
            emit_phase_e(p, io, hb, y)

        if debug_taps:
            for tname in debug_taps:
                ap = hb[tname]
                shp = [int(s) for s in ap.tensor.shape]
                n = int(np.prod(shp))
                out = nc.dram_tensor(f"dbg_{tname}", [n], F32,
                                     kind="ExternalOutput").ap()
                src = ap.tensor.ap()
                dims = " ".join(f"d{i}" for i in range(len(shp)))
                flat_in = src.rearrange(f"{dims} -> ({dims})")
                p.wait_all("gpsimd", "st")
                p.wait_all("gpsimd", "cc")
                CH = 1 << 20
                for off in range(0, n, CH):
                    m = min(CH, n - off)
                    p.dma("gpsimd", out[off:off + m], flat_in[off:off + m])
            p.wait_all("gpsimd", "st")
            p.flush()

        if "e" not in phases:
            p.wait_all("gpsimd", "st")
            for t in range(NT):
                p.dma("gpsimd", y[t * 128:(t + 1) * 128, :],
                      io["x0"][t * 128:(t + 1) * 128, :])
            p.wait_all("gpsimd", "st")
            p.flush()

    return nc



def make_in_maps(inputs):
    masks = _carry_masks()
    x_flat = np.asarray(inputs["L0_attn_x"], np.float32).reshape(B * T, D)
    shared = {
        "wq_a": np.asarray(inputs["L0_attn_wq_a"], np.float32),
        "gq_rep": np.tile(np.asarray(inputs["L0_attn_q_a_norm_w"],
                                     np.float32)[None, :], (128, 1)),
        "wq_b": np.asarray(inputs["L0_attn_wq_b"], np.float32),
        "wkv_a": np.asarray(inputs["L0_attn_wkv_a"], np.float32),
        "gkv_rep": np.tile(np.asarray(inputs["L0_attn_kv_a_norm_w"],
                                      np.float32)[None, :], (128, 1)),
        "wkv_b": np.asarray(inputs["L0_attn_wkv_b"], np.float32),
        "wo": np.asarray(inputs["L0_attn_wo"], np.float32),
        "w_gate": np.asarray(inputs["L0_ffn_w_gate"], np.float32),
        "w_sup": np.asarray(inputs["L0_ffn_w_shared_up"], np.float32),
        "w_sdn": np.asarray(inputs["L0_ffn_w_shared_down"], np.float32),
        "w_up": np.asarray(inputs["L0_ffn_w_up"], np.float32),
        "w_dn": np.asarray(inputs["L0_ffn_w_down"], np.float32),
        "w_qkv1T": np.ascontiguousarray(
            np.asarray(inputs["L1_attn_w_qkv"], np.float32).T),
        "b_qkv1_rep": np.tile(np.asarray(inputs["L1_attn_b_qkv"],
                                         np.float32)[None, :], (128, 1)),
        "w_out1T": np.ascontiguousarray(
            np.asarray(inputs["L1_attn_w_out"], np.float32).T),
        "b_out1_rep": np.tile(np.asarray(inputs["L1_attn_b_out"],
                                         np.float32)[None, :], (128, 1)),
        "w_gate2T": np.ascontiguousarray(
            np.asarray(inputs["L1_ffn_W_gate"], np.float32).T),
        "w_qkv2T": np.ascontiguousarray(
            np.asarray(inputs["L1_ffn_W_qkv"], np.float32).T),
        "identity": np.eye(128, dtype=np.float32),
        "ut_strict": np.triu(np.ones((128, 128), np.float32), 1),
        "iota256_rep": np.tile(np.arange(256, dtype=np.float32)[None, :],
                               (128, 1)),
        "iota64_rep": np.tile(np.arange(64, dtype=np.float32)[None, :],
                              (128, 1)),
    }
    in_maps = []
    for c in range(NCORE):
        rows = _core_rows(c)
        cosT, sinT = _rope_tables(rows % T)
        m = dict(shared)
        m["x0"] = np.ascontiguousarray(x_flat[rows])
        m["cosT"] = cosT
        m["sinT"] = sinT
        m["carry_maskT"] = np.ascontiguousarray(masks[c].T)
        m["pcol"] = np.arange(128, dtype=np.float32).reshape(128, 1)
        m["b1col"] = np.ascontiguousarray(
            np.asarray(inputs["L1_attn_b_qkv"], np.float32).reshape(48, 128).T)
        tiles = _core_tiles(c)
        kid_seq = np.concatenate(
            [np.arange(g * 128, (g + 1) * 128) for r in range(4)
             for g in _core_tiles((c // 4) * 4 + r)]).astype(np.float32)
        m["kid"] = np.ascontiguousarray(kid_seq.reshape(64, 128).T)
        qid_vals = np.concatenate(
            [np.arange(g * 128, (g + 1) * 128) for g in tiles]).astype(np.float32)
        m["qid_rep"] = np.tile(qid_vals[None, :], (128, 1))
        in_maps.append(m)
    return in_maps


def kernel(**inputs):
    in_maps = make_in_maps(inputs)
    nc = build_program()
    res = run_bass_kernel_spmd(nc, in_maps, core_ids=list(range(NCORE)))
    out = np.zeros((B * T, D), np.float32)
    for c in range(NCORE):
        out[_core_rows(c)] = res.results[c]["y"]
    return out.reshape(B, T, D)
